# version 21
# baseline (speedup 1.0000x reference)
import numpy as np
from contextlib import ExitStack

import concourse.bass as bass
import concourse.mybir as mybir
from concourse.bass_utils import run_bass_kernel_spmd

F32 = mybir.dt.float32
F32R = mybir.dt.float32r
BF16 = mybir.dt.bfloat16
AF = mybir.ActivationFunctionType
ALU = mybir.AluOpType
AX = mybir.AxisListType

NCORES = 8
NSEQ = 4
IBC = 512
NIBT = 4
NT = NSEQ * NIBT
CM = 1024
KC = 8
NH = 16
HD = 64
SCALE = float(HD) ** -0.5

G_QKV, G_SEL, G_CMP, G_PROJ = 6, 16, 16, 2


class Sched:

    ENGINES = ("sp", "pe", "act", "dve", "pool")

    def __init__(self):
        self.items = {e: [] for e in self.ENGINES}
        self.ctr = {}
        self.seq = 0
        self.mark = {}

    DMA_ROT = {"dx": 8, "dw": 16, "dout": 8}

    def emit(self, eng, fn, waits=(), inc=None, inc_amt=1):
        tok = None
        if inc is not None:
            if inc in self.DMA_ROT:
                n = self.ctr.get(inc, 0)
                self.ctr[inc] = n + 1
                depth = self.DMA_ROT[inc]
                inc = f"{inc}{n % depth}"
                inc_amt = 16
                self.ctr[inc] = self.ctr.get(inc, 0) + 16
            else:
                self.ctr[inc] = self.ctr.get(inc, 0) + inc_amt
            tok = (inc, self.ctr[inc])
        ws = [w for w in waits if w is not None]
        self.items[eng].append((fn, ws, (inc, inc_amt) if inc else None,
                                self.seq))
        self.seq += 1
        return tok


def build_program(taps=False, serial=False, serial_until=0):
    nc = bass.Bass("TRN2", target_bir_lowering=False)

    x_h = nc.dram_tensor("x", [NSEQ, IBC, CM], F32, kind="ExternalInput").ap()
    wq_h = nc.dram_tensor("wq", [G_QKV, 128, 4096], F32, kind="ExternalInput").ap()
    ws_h = nc.dram_tensor("ws", [G_SEL, 128, 4096], F32, kind="ExternalInput").ap()
    wc_h = nc.dram_tensor("wc", [G_CMP, 128, 4096], F32, kind="ExternalInput").ap()
    wp_h = nc.dram_tensor("wp", [G_PROJ, 128, 4096], F32, kind="ExternalInput").ap()
    b2_h = nc.dram_tensor("b2", [16, 1024], F32, kind="ExternalInput").ap()
    b3_h = nc.dram_tensor("b3", [1, 4096], F32, kind="ExternalInput").ap()
    out_h = nc.dram_tensor("out", [NSEQ, IBC, CM], F32, kind="ExternalOutput").ap()
    if taps:
        tap_xn = nc.dram_tensor("tap_xn", [128, CM], F32, kind="ExternalOutput").ap()
        tap_qkv = nc.dram_tensor("tap_qkv", [128, 1536], F32, kind="ExternalOutput").ap()
        tap_a = nc.dram_tensor("tap_a", [128, 128], F32, kind="ExternalOutput").ap()
        tap_g = nc.dram_tensor("tap_g", [128, CM], F32, kind="ExternalOutput").ap()

    s = Sched()

    pad_prev = [None]
    plpad_prev = [None]

    def emit_pool_pad():
        tok = s.emit("pool", lambda e: e.memset(plpad[:], 1.0),
                     waits=[plpad_prev[0]], inc="pool")
        plpad_prev[0] = tok
        return tok

    def emit_pad():
        tok = s.emit("act", lambda e: e.copy(out=padt[:], in_=ones_f[:, 0:1]),
                     waits=[t_ones, pad_prev[0]], inc="act")
        pad_prev[0] = tok
        return tok

    def later(tok, n=1):
        return None if tok is None else (tok[0], tok[1] + n)

    es0 = ExitStack()
    es12 = ExitStack()

    ones_f = es0.enter_context(nc.sbuf_tensor("ones_f", [128, 128], F32))
    ones_r = es0.enter_context(nc.sbuf_tensor("ones_r", [128, 128], F32R))
    ident = es0.enter_context(nc.sbuf_tensor("ident", [128, 128], BF16))
    eps_t = es0.enter_context(nc.sbuf_tensor("eps_t", [128, 1], F32))
    padt = es0.enter_context(nc.sbuf_tensor("padt", [128, 1], F32))
    plpad = es0.enter_context(nc.sbuf_tensor("plpad", [128, 1], F32))
    bias_r = es0.enter_context(nc.sbuf_tensor("bias_r", [128, 3072], F32R))
    xnT = es0.enter_context(nc.sbuf_tensor("xnT", [128, KC, 2048], BF16))
    Fg = es0.enter_context(nc.sbuf_tensor("Fg", [128, NT, CM], BF16))
    wqb = es0.enter_context(nc.sbuf_tensor("wqb", [128, KC, 3072], BF16))
    stag3 = es0.enter_context(nc.sbuf_tensor("stag3", [128, 1024], F32))

    ps = [es0.enter_context(nc.psum_tensor(f"ps{i}", [128, 512], F32))
          for i in range(5)]
    pt2 = [es0.enter_context(nc.psum_tensor(f"pt{i}", [128, 1024], BF16))
           for i in range(3)]

    sems = {}
    sem_names = ["pe", "act", "dve", "pool", "chain"]
    for base, depth in Sched.DMA_ROT.items():
        sem_names += [f"{base}{i}" for i in range(depth)]
    for name in sem_names:
        sems[name] = es0.enter_context(nc.semaphore(name))

    wscb = es12.enter_context(nc.sbuf_tensor("wscb", [128, 2, 2, KC, 512], BF16))
    stag = es12.enter_context(nc.sbuf_tensor("stag", [128, 2048], F32))
    miscF = es12.enter_context(nc.sbuf_tensor("miscF", [128, 6144], F32))
    miscB = es12.enter_context(nc.sbuf_tensor("miscB", [128, 4096], BF16))
    bn6 = es12.enter_context(nc.sbuf_tensor("bn6", [128, 12], F32))
    mv = es12.enter_context(nc.sbuf_tensor("mv", [128, 2], F32))
    rstd = es12.enter_context(nc.sbuf_tensor("rstd", [128, 1], F32))
    wsb = wscb[:, 0]
    wcb = wscb[:, 1]
    xt = miscF[:, 0:2048].rearrange("p (b c) -> p b c", c=CM)
    bstag = miscF[:, 2048:3072]
    selM = miscF[:, 3072:3136]
    xnb = miscB[:, 0:1024]
    selK = miscB[:, 1024:1536]
    selP = miscB[:, 1536:2048]

    t_ones = s.emit("pool", lambda e: e.memset(ones_f[:], 1.0), inc="pool")
    t_eps = s.emit("pool", lambda e: e.memset(eps_t[:], 1e-5), inc="pool")
    t_identz = s.emit("pool", lambda e: e.memset(ident[:], 0.0), inc="pool")
    t_ident = s.emit("pool", lambda e: e.affine_select(
        out=ident[:], in_=ident[:], compare_op=ALU.not_equal, fill=1.0,
        base=0, pattern=[[-1, 128]], channel_multiplier=1),
        waits=[t_identz], inc="pool")
    t_onesr = s.emit("dve", lambda e: e.tensor_copy(out=ones_r[:], in_=ones_f[:]),
                     waits=[t_ones], inc="dve")



    s.mark["ph1"] = s.seq
    xnT_tok = [None] * NT
    xt_free = [None, None]
    tr_free = [None] * 3
    trc = 0

    def pt_slot(sl):
        return pt2[sl % 3][:, 0:128]
    prev_sq = None
    prev_xn = None
    last_tr_pe = None
    for t in range(NT):
        nn, ibt = t // NIBT, t % NIBT
        par = t % 2
        d = s.emit("sp", lambda e, nn=nn, ibt=ibt, par=par: e.dma_start(
            out=xt[:, par, :], in_=x_h[nn, ibt * 128:(ibt + 1) * 128, :]),
            waits=[xt_free[par]], inc="dx", inc_amt=16)
        s.emit("dve", lambda e, par=par: e.bn_stats(bn6[:, 0:6], xt[:, par, 0:512]),
               waits=[d], inc="dve")
        t_st2 = s.emit("dve", lambda e, par=par: e.bn_stats(
            bn6[:, 6:12], xt[:, par, 512:1024]), inc="dve")
        t_ag = s.emit("dve", lambda e: e.bn_aggr(mv[:], bn6[:]),
                      waits=[prev_sq, t_st2], inc="dve")
        t_sq = s.emit("act", lambda e: e.activation(rstd[:], mv[:, 1:2], AF.Sqrt,
                                                    bias=eps_t[:]),
                      waits=[t_ag, prev_xn, t_eps], inc="act")
        prev_sq = t_sq
        t_rc = s.emit("dve", lambda e: e.reciprocal(rstd[:], rstd[:]),
                      waits=[t_sq], inc="dve")
        t_xn = s.emit("dve", lambda e, par=par: e.tensor_scalar(
            out=xnb[:], in0=xt[:, par, :], scalar1=mv[:, 0:1], scalar2=rstd[:],
            op0=ALU.subtract, op1=ALU.mult), waits=[last_tr_pe, t_rc],
            inc="dve")
        prev_xn = t_xn
        xt_free[par] = t_xn
        for c in range(KC):
            sl = trc % 3
            t_tr = s.emit("pe", lambda e, c=c, sl=sl: e.transpose(
                pt_slot(sl), xnb[:, c * 128:(c + 1) * 128], ident[:]),
                waits=[t_xn, t_ident, tr_free[sl]], inc="pe")
            last_tr_pe = t_tr
            xnT_tok[t] = s.emit("act", lambda e, c=c, t=t, sl=sl: e.copy(
                out=xnT[:, c, t * 128:(t + 1) * 128], in_=pt_slot(sl)),
                waits=[later(t_tr, 2)], inc="act")
            tr_free[sl] = emit_pad()
            trc += 1
        for _ in range(2):
            sl = trc % 3
            last_tr_pe = s.emit("pe", lambda e, sl=sl: e.transpose(
                pt_slot(sl), ident[:], ident[:]),
                waits=[tr_free[sl]], inc="pe")
            trc += 1
    tap_toks = []
    if taps:
        tap_toks.append(s.emit("pool", lambda e: e.dma_start(
            out=tap_xn, in_=xnb[:]),
            waits=[xnT_tok[NT - 1]], inc="dout", inc_amt=16))

    s.mark["ph2"] = s.seq
    st3_free = None

    def load_w3_quarter(dst, srch, g, q, stag, dg=None, extra=()):
        nonlocal st3_free
        if dg is None:
            dg = g
        d = s.emit("sp", lambda e, g=g, q=q: e.dma_start(
            out=stag[:], in_=srch[g][:, q * 1024:(q + 1) * 1024]),
            waits=[st3_free] + list(extra), inc="dw", inc_amt=16)
        tok = s.emit("act", lambda e, dg=dg, q=q: e.copy(
            out=dst[:, q * 2:(q + 1) * 2, dg * 512:(dg + 1) * 512],
            in_=stag[:].rearrange("p (k j) -> p k j", j=512)),
            waits=[d], inc="act")
        st3_free = emit_pad()
        return tok

    wq_tok = [None] * G_QKV
    wq_quarters = [(g, q) for g in range(G_QKV) for q in range(4)]

    pe_g_tok = {}
    sel_bank_free = [None, None]
    stag_free = None
    bstag_free = None
    f_tok = [None] * NT

    for g in range(G_SEL):
        wpar = g % 2
        db = s.emit("sp", lambda e, g=g: e.dma_start(out=bstag[0:1, :],
                                                     in_=b2_h[g:g + 1, :]),
                    waits=[bstag_free], inc="dw", inc_amt=16)
        cb = s.emit("dve", lambda e, wpar=wpar: e.tensor_copy(
            out=bias_r[0:1, wpar * 1024:wpar * 1024 + 1024],
            in_=bstag[0:1, :]),
            waits=[db, pe_g_tok.get(g - 2)], inc="dve")
        bstag_free = later(cb)
        c1 = c2 = None
        for mat, wh, wbuf in ((0, ws_h, wsb), (1, wc_h, wcb)):
            for half in range(2):
                d1 = s.emit("sp", lambda e, g=g, half=half, wh=wh: e.dma_start(
                    out=stag[:], in_=wh[g][:, half * 2048:(half + 1) * 2048]),
                    waits=[stag_free], inc="dw", inc_amt=16)
                cc = s.emit("act", lambda e, wpar=wpar, half=half, wbuf=wbuf: e.copy(
                    out=wbuf[:, wpar, half * 4:(half + 1) * 4, :],
                    in_=stag[:].rearrange("p (k j) -> p k j", j=512)),
                    waits=[d1, pe_g_tok.get(g - 2)], inc="act")
                stag_free = emit_pad()
            if mat == 0:
                c1 = cc
            else:
                c2 = cc
        for gq in wq_quarters[2 * g:2 * g + 2]:
            wq_tok[gq[0]] = load_w3_quarter(wqb, wq_h, gq[0], gq[1], stag3)

        cS = slice(wpar * 1024, wpar * 1024 + 512)
        cC = slice(wpar * 1024 + 512, wpar * 1024 + 1024)
        for t in range(NT):
            gi = g * NT + t
            par = gi % 2
            psS, psC = ps[2 * par], ps[2 * par + 1]
            for k in range(KC):
                tok = s.emit("pe", lambda e, k=k, t=t, wpar=wpar, psS=psS: e.matmul(
                    psS[:], xnT[:, k, t * 128:(t + 1) * 128], wsb[:, wpar, k, :],
                    start=(k == 0), stop=False),
                    waits=[later(c1), later(xnT_tok[t]),
                           later(sel_bank_free[par]), t_onesr]
                    if k == 0 else (), inc="pe")
            tS = s.emit("pe", lambda e, cS=cS, psS=psS: e.matmul(
                psS[:], ones_r[0:1, :], bias_r[0:1, cS],
                start=False, stop=True), waits=[cb], inc="pe")
            for k in range(KC):
                tok = s.emit("pe", lambda e, k=k, t=t, wpar=wpar, psC=psC: e.matmul(
                    psC[:], xnT[:, k, t * 128:(t + 1) * 128], wcb[:, wpar, k, :],
                    start=(k == 0), stop=False),
                    waits=[later(c2)] if k == 0 else (), inc="pe")
            tC = s.emit("pe", lambda e, cC=cC, psC=psC: e.matmul(
                psC[:], ones_r[0:1, :], bias_r[0:1, cC],
                start=False, stop=True), inc="pe")
            pe_g_tok[g] = tC
            t_M = s.emit("dve", lambda e, psS=psS: e.tensor_reduce(
                out=selM[:], in_=psS[:].rearrange("p (o w) -> p o w", w=8),
                axis=AX.X, op=ALU.max), waits=[later(tS)], inc="dve")
            t_K = s.emit("dve", lambda e, psS=psS: e.tensor_tensor(
                out=selK[:].rearrange("p (o w) -> p o w", w=8),
                in0=psS[:].rearrange("p (o w) -> p o w", w=8),
                in1=selM[:].unsqueeze(2).broadcast_to((128, 64, 8)),
                op=ALU.is_ge), waits=[t_M], inc="dve")
            t_P = s.emit("dve", lambda e, psC=psC: e.scalar_tensor_tensor(
                out=selP[:], in0=selK[:], scalar=1.0,
                in1=psC[:], op0=ALU.mult, op1=ALU.mult),
                waits=[later(tC), t_K], inc="dve")
            def _fred(e, t=t, g=g):
                with nc.allow_low_precision(reason="hard-selection gate in bf16"):
                    return e.tensor_reduce(
                        out=Fg[:, t, g * 64:(g + 1) * 64],
                        in_=selP[:].rearrange("p (o w) -> p o w", w=8),
                        axis=AX.X, op=ALU.add)
            tF = s.emit("dve", _fred, waits=[t_P], inc="dve")
            sel_bank_free[par] = tF
            f_tok[t] = tF

    sl = trc % 3
    s.emit("pe", lambda e, sl=sl: e.transpose(
        pt_slot(sl), ident[:], ident[:]), waits=[tr_free[sl]], inc="pe")
    trc += 1

    g_tok = [None] * NT
    for t in range(NT):
        g_tok[t] = s.emit("act", lambda e, t=t: e.activation(
            Fg[:, t, :], Fg[:, t, :], AF.Silu), waits=[f_tok[t]], inc="act")
    if taps:
        s.emit("pool", lambda e: e.dma_start(out=tap_g, in_=Fg[:, 0, :]),
               waits=[g_tok[0]], inc="dout", inc_amt=16)

    ph2_done = [pe_g_tok[G_SEL - 1], sel_bank_free[0], sel_bank_free[1],
                g_tok[NT - 1]]

    s.mark["ph3"] = s.seq
    es12.close()
    es3a = ExitStack()
    es3b = ExitStack()
    attnT = es3b.enter_context(nc.sbuf_tensor("attnT", [128, KC, 2048], BF16))
    qkv = es3a.enter_context(nc.sbuf_tensor("qkv", [128, 2, NSEQ, 1536], BF16))
    aout = es3a.enter_context(nc.sbuf_tensor("aout", [128, 2, NSEQ, 512], BF16))
    scF = es3a.enter_context(nc.sbuf_tensor("scF", [128, 512], F32))
    scA = es3a.enter_context(nc.sbuf_tensor("scA", [128, 128], BF16))
    pav = es3a.enter_context(nc.sbuf_tensor("pav", [128, 6144], BF16))
    sc = scF[:, 0:128]
    scD = scF[:, 128:256]
    scE = scF[:, 256:384]
    scM = scF[:, 384:416]
    scZ = scF[:, 416:448]
    scR = scF[:, 448:480]
    prod = [pav[:, 0:2048], pav[:, 2048:4096]]
    avm = pav[:, 4096:6144]

    t_bias3 = None
    for r in range(3):
        d = s.emit("sp", lambda e, r=r: e.dma_start(
            out=stag3[0:1, :], in_=b3_h[0:1, r * 1024:(r + 1) * 1024]),
            waits=([t_bias3] if r else []) + [st3_free],
            inc="dw", inc_amt=16)
        t_bias3 = s.emit("dve", lambda e, r=r: e.tensor_copy(
            out=bias_r[:, r * 1024:(r + 1) * 1024], in_=stag3[:]),
            waits=[d, pe_g_tok[G_SEL - 1]], inc="dve")
        st3_free = t_bias3

    rot = [sel_bank_free[0], sel_bank_free[0],
           sel_bank_free[1], sel_bank_free[1], None]
    rotc = 0
    qkv_free = [None, None]
    aout_free = [None, None]
    prod_free = [None, None]
    avm_free = None
    scE_free = [None] * NSEQ
    prev_tE = [None] * NSEQ
    wp_tok = [None] * G_PROJ
    pending_tr = []
    last_copy = None
    first_pool = list(sel_bank_free)

    H2 = 8
    sc4 = sc[:].rearrange("p (n h m) -> p n h m", n=NSEQ, h=H2, m=NSEQ)
    scA4 = scA[:].rearrange("p (n h m) -> p n h m", n=NSEQ, h=H2, m=NSEQ)

    def flush_transposes():
        nonlocal pending_tr, trc, last_copy
        if not pending_tr:
            return
        it_f = pending_tr[0][3]
        for ta, t, hh_, it_ in pending_tr:
            n_ = t // NIBT
            for j in range(4):
                sl = trc % 3
                t_tr = s.emit("pe", lambda e, it_=it_, n_=n_, j=j, sl=sl:
                              e.transpose(
                    pt_slot(sl),
                    aout[:, it_, n_, j * 128:(j + 1) * 128],
                    ident[:]), waits=[ta, tr_free[sl]], inc="pe")
                last_copy = s.emit(
                    "act", lambda e, t=t, j=j, hh_=hh_, sl=sl: e.copy(
                        out=attnT[:, hh_ * 4 + j, t * 128:(t + 1) * 128],
                        in_=pt_slot(sl)), waits=[later(t_tr, 2)],
                    inc="act")
                tr_free[sl] = emit_pad()
                trc += 1
        for _ in range(2):
            sl = trc % 3
            s.emit("pe", lambda e, sl=sl: e.transpose(
                pt_slot(sl), ident[:], ident[:]),
                waits=[tr_free[sl]], inc="pe")
            trc += 1
        aout_free[it_f] = last_copy
        pending_tr = []

    for ibt in range(NIBT):
        for hh in range(2):
            it = (ibt * 2 + hh) % 2
            e_k_last = None
            e_q = [None] * NSEQ
            for comp in (1, 0, 2):
                fo = comp * 2 + hh
                for nn in range(NSEQ):
                    t = nn * NIBT + ibt
                    bank = rotc % 5
                    rotc += 1
                    for k in range(KC):
                        s.emit("pe", lambda e, k=k, t=t, fo=fo, bank=bank: e.matmul(
                            ps[bank][:], xnT[:, k, t * 128:(t + 1) * 128],
                            wqb[:, k, fo * 512:(fo + 1) * 512],
                            start=(k == 0), stop=False),
                            waits=[later(wq_tok[fo]), later(rot[bank])]
                            if k == 0 else (), inc="pe")
                    tq_ = s.emit("pe", lambda e, fo=fo, bank=bank: e.matmul(
                        ps[bank][:], ones_r[0:1, :],
                        bias_r[0:1, fo * 512:(fo + 1) * 512],
                        start=False, stop=True), waits=[t_bias3], inc="pe")
                    ev = rot[bank] = s.emit(
                        "act", lambda e, it=it, nn=nn, comp=comp, bank=bank: e.copy(
                            out=qkv[:, it, nn, comp * 512:(comp + 1) * 512],
                            in_=ps[bank][:]),
                        waits=[later(tq_, 2)] + (qkv_free[it] or []), inc="act")
                    if comp == 1:
                        e_k_last = ev
                    elif comp == 0:
                        e_q[nn] = ev
            evac_last = rot[(rotc - 1) % 5]
            if ibt == NIBT - 1 and hh == 0:
                last_mm_evenfo = tq_

            flush_transposes()

            emit_pad()
            last_qk_pool = None
            for n_ in range(NSEQ):
                k_all = qkv[:, it, :, 512:1024].rearrange(
                    "p m (h d) -> p m h d", d=HD)
                q_n = qkv[:, it, n_, 0:512].rearrange(
                    "p (h d) -> p h d", d=HD).unsqueeze(1).broadcast_to(
                    (128, NSEQ, H2, HD))
                pp = n_ % 2
                tq = s.emit("pool", lambda e, pp=pp, k_all=k_all, q_n=q_n:
                            e.tensor_tensor(
                    out=prod[pp].rearrange("p (m h d) -> p m h d",
                                           m=NSEQ, d=HD),
                    in0=k_all, in1=q_n, op=ALU.mult),
                    waits=[later(e_k_last), later(e_q[n_]), prod_free[pp]]
                    + first_pool + tap_toks, inc="pool")
                first_pool = []
                last_qk_pool = tq
                td = s.emit("dve", lambda e, pp=pp, n_=n_: e.tensor_reduce(
                    out=sc4[:, n_, :, :].transpose([0, 2, 1]),
                    in_=prod[pp].rearrange("p (m h d) -> p m h d",
                                           m=NSEQ, d=HD),
                    axis=AX.X, op=ALU.add), waits=[later(tq)], inc="dve")
                prod_free[pp] = td
                if n_ == NSEQ - 1:
                    emit_pool_pad()

                cs = slice(n_ * 8, (n_ + 1) * 8)
                cs4 = slice(n_ * 32, (n_ + 1) * 32)
                t_M4 = s.emit("dve", lambda e, n_=n_, cs=cs: e.tensor_reduce(
                    out=scM[:, cs], in_=sc4[:, n_, :, :],
                    axis=AX.X, op=ALU.max), waits=[td], inc="dve")
                tD = s.emit("dve", lambda e, n_=n_, cs=cs, cs4=cs4:
                            e.tensor_tensor(
                    out=scD[:, cs4].rearrange("p (h m) -> p h m", m=NSEQ),
                    in0=sc4[:, n_, :, :],
                    in1=scM[:, cs].unsqueeze(2).broadcast_to((128, 8, NSEQ)),
                    op=ALU.subtract),
                    waits=[prev_tE[n_], t_M4], inc="dve")
                tE = s.emit("act", lambda e, cs4=cs4: e.activation(
                    scE[:, cs4], scD[:, cs4], AF.Exp, scale=SCALE),
                    waits=[tD, scE_free[n_]], inc="act")
                tEp = emit_pad()
                prev_tE[n_] = tEp
                t_Z = s.emit("dve", lambda e, cs=cs, cs4=cs4: e.tensor_reduce(
                    out=scZ[:, cs],
                    in_=scE[:, cs4].rearrange("p (h m) -> p h m", m=NSEQ),
                    axis=AX.X, op=ALU.add), waits=[tEp], inc="dve")
                t_R = s.emit("dve", lambda e, cs=cs: e.reciprocal(
                    scR[:, cs], scZ[:, cs]), waits=[t_Z], inc="dve")
                tA = s.emit("dve", lambda e, n_=n_, cs=cs, cs4=cs4:
                            e.tensor_tensor(
                    out=scA4[:, n_, :, :],
                    in0=scE[:, cs4].rearrange("p (h m) -> p h m", m=NSEQ),
                    in1=scR[:, cs].unsqueeze(2).broadcast_to((128, 8, NSEQ)),
                    op=ALU.mult), waits=[t_R], inc="dve")
                scE_free[n_] = tA

                v_all = qkv[:, it, :, 1024:1536].rearrange(
                    "p m (h d) -> p m h d", d=HD)
                a_n = scA4[:, n_, :, :].transpose([0, 2, 1]).unsqueeze(
                    3).broadcast_to((128, NSEQ, H2, HD))
                tm = s.emit("dve", lambda e, v_all=v_all, a_n=a_n:
                            e.tensor_tensor(
                    out=avm[:].rearrange("p (m h d) -> p m h d",
                                         m=NSEQ, d=HD),
                    in0=v_all, in1=a_n, op=ALU.mult),
                    waits=[avm_free, tA, later(evac_last)], inc="dve")

                def _avred(e, it=it, n_=n_):
                    with nc.allow_low_precision(reason="attention out bf16"):
                        return e.tensor_reduce(
                            out=aout[:, it, n_, :],
                            in_=avm[:].rearrange("p (m f) -> p f m", f=512),
                            axis=AX.X, op=ALU.add)
                ta = s.emit("dve", _avred, waits=[tm, later(aout_free[it])],
                            inc="dve")
                avm_free = ta
                t = n_ * NIBT + ibt
                pending_tr.append((ta, t, hh, it))
            qkv_free[it] = [later(last_qk_pool), avm_free]
            if taps and ibt == 0 and hh == 0:
                tap_toks.append(s.emit("pool", lambda e: e.dma_start(
                    out=tap_qkv, in_=qkv[:, 0, 0, :]),
                    waits=[avm_free], inc="dout", inc_amt=16))
                tap_toks.append(s.emit("pool", lambda e: e.dma_start(
                    out=tap_a, in_=scA[:]),
                    waits=[avm_free], inc="dout", inc_amt=16))
                qkv_free[it] = qkv_free[it] + tap_toks
                scE_free = [(t if t is None else max(t, tap_toks[-1],
                             key=lambda x: 0 if x is None else 1))
                            for t in scE_free]
                prev_tE = [tap_toks[-1] if t is None else t for t in prev_tE]
            if ibt == NIBT - 1 and hh == 1:
                for g in range(G_PROJ):
                    for q in range(4):
                        wp_tok[g] = load_w3_quarter(
                            wqb, wp_h, g, q, stag3, dg=2 * g,
                            extra=[later(last_mm_evenfo, 2)])

    flush_transposes()
    attnT_tok = [last_copy]
    ph3_done = [later(last_copy), qkv_free[0][0], qkv_free[0][1],
                qkv_free[1][0], qkv_free[1][1]]

    s.mark["ph4"] = s.seq
    es3a.close()
    es4 = ExitStack()
    xre = es4.enter_context(nc.sbuf_tensor("xre", [128, 2, CM], F32))
    t0b = es4.enter_context(nc.sbuf_tensor("t0b", [128, 512], F32))
    outs = es4.enter_context(nc.sbuf_tensor("outs", [128, 2, CM], F32))

    d = s.emit("sp", lambda e: e.dma_start(
        out=stag3[0:1, :], in_=b3_h[0:1, 3072:4096]),
        waits=ph3_done + [st3_free], inc="dw", inc_amt=16)
    t_bias4 = s.emit("dve", lambda e: e.tensor_copy(
        out=bias_r[:, 0:1024], in_=stag3[:]), waits=[d], inc="dve")
    st3_free = t_bias4

    xre_free = [None, None]
    outs_free = [None, None]
    t0b_prev = [None]
    for t in range(NT):
        nn, ibt = t // NIBT, t % NIBT
        par = t % 2
        dx = s.emit("sp", lambda e, nn=nn, ibt=ibt, par=par: e.dma_start(
            out=xre[:, par, :], in_=x_h[nn, ibt * 128:(ibt + 1) * 128, :]),
            waits=[xre_free[par]], inc="dx", inc_amt=16)
        for gp in range(2):
            bank = rotc % 5
            rotc += 1
            for c in range(KC):
                s.emit("pe", lambda e, c=c, t=t, gp=gp, bank=bank: e.matmul(
                    ps[bank][:], attnT[:, c, t * 128:(t + 1) * 128],
                    wqb[:, c, gp * 1024:gp * 1024 + 512],
                    start=(c == 0), stop=False),
                    waits=[later(x) for x in attnT_tok]
                    + [later(wp_tok[gp]), later(rot[bank])]
                    if c == 0 else (), inc="pe")
            tp_ = s.emit("pe", lambda e, gp=gp, bank=bank: e.matmul(
                ps[bank][:], ones_r[0:1, :],
                bias_r[0:1, gp * 512:(gp + 1) * 512],
                start=False, stop=True), waits=[t_bias4], inc="pe")
            t_stt = s.emit("dve", lambda e, t=t, gp=gp, bank=bank: e.scalar_tensor_tensor(
                out=t0b[:], in0=Fg[:, t, gp * 512:(gp + 1) * 512],
                scalar=1.0, in1=ps[bank][:], op0=ALU.mult, op1=ALU.mult),
                waits=[later(tp_), later(g_tok[t]), t0b_prev[0]], inc="dve")
            rot[bank] = tout = s.emit("dve", lambda e, par=par, gp=gp: e.tensor_tensor(
                out=outs[:, par, gp * 512:(gp + 1) * 512],
                in0=xre[:, par, gp * 512:(gp + 1) * 512], in1=t0b[:],
                op=ALU.add), waits=[dx, outs_free[par], t_stt], inc="dve")
            t0b_prev[0] = tout
        xre_free[par] = tout
        od = s.emit("act", lambda e, nn=nn, ibt=ibt, par=par: e.dma_start(
            out=out_h[nn, ibt * 128:(ibt + 1) * 128, :], in_=outs[:, par, :]),
            waits=[tout], inc="dout", inc_amt=16)
        outs_free[par] = od

    emit_pool_pad()
    sl = trc % 3
    s.emit("pe", lambda e, sl=sl: e.transpose(pt_slot(sl), ident[:], ident[:]),
           waits=[tr_free[sl]], inc="pe")

    s.mark["end"] = s.seq
    douts = [(f"dout{i}", s.ctr[f"dout{i}"]) for i in range(s.DMA_ROT["dout"])
             if f"dout{i}" in s.ctr]
    s.emit("sp", lambda e: e.nop(), waits=douts)

    if serial:
        serial_until = s.seq + 1
    pre_ctr = {}
    for eng_name in s.ENGINES:
        for fn, waits, inc, seq in s.items[eng_name]:
            if seq < serial_until and inc is not None:
                pre_ctr[inc[0]] = pre_ctr.get(inc[0], 0) + inc[1]
    with nc.Block() as block:
        def make_body(eng_name):
            def body(eng):
                last_wait = {}
                for fn, waits, inc, seq in s.items[eng_name]:
                    if seq < serial_until:
                        if seq > 0:
                            eng.wait_ge(sems["chain"], 16 * seq)
                        inst = fn(eng)
                        inst.then_inc(sems["chain"], 16)
                        continue
                    if serial_until and last_wait.get("chain", 0) == 0:
                        eng.wait_ge(sems["chain"], 16 * serial_until)
                        last_wait["chain"] = 16 * serial_until
                    for sem_name, val in waits:
                        val -= pre_ctr.get(sem_name, 0)
                        if val > last_wait.get(sem_name, 0):
                            eng.wait_ge(sems[sem_name], val)
                            last_wait[sem_name] = val
                    inst = fn(eng)
                    if inc is not None:
                        inst.then_inc(sems[inc[0]], inc[1])
            return body
        block.sync(make_body("sp"))
        block.tensor(make_body("pe"))
        block.scalar(make_body("act"))
        block.vector(make_body("dve"))
        block.gpsimd(make_body("pool"))

    es4.close()
    es3b.close()
    es0.close()
    nc._sched = s
    return nc


def _prep_inputs(L0_attn_x, L0_attn_Wqkv, L0_attn_bqkv, L0_attn_Wproj,
                 L0_attn_bproj, L0_ffn_sel_w, L0_ffn_sel_b, L0_ffn_comp_w,
                 L0_ffn_comp_b):
    f32 = np.float32

    def pieces(W):
        WT = np.ascontiguousarray(np.asarray(W, dtype=f32).T)
        G = WT.shape[1] // 512
        arr = WT.reshape(KC, 128, G, 512).transpose(2, 1, 0, 3)
        return np.ascontiguousarray(arr.reshape(G, 128, 4096))

    wq = pieces(L0_attn_Wqkv)
    ws = pieces(L0_ffn_sel_w)
    wcat = np.asarray(L0_ffn_comp_w, dtype=f32).transpose(1, 0, 2).reshape(8192, 1024)
    wc = pieces(wcat)
    wp = pieces(L0_attn_Wproj)

    selb = np.asarray(L0_ffn_sel_b, dtype=f32)
    cmpb = np.asarray(L0_ffn_comp_b, dtype=f32).T.reshape(-1)
    b2 = np.stack([np.concatenate([selb[g * 512:(g + 1) * 512],
                                   cmpb[g * 512:(g + 1) * 512]])
                   for g in range(16)])
    b3 = np.concatenate([np.asarray(L0_attn_bqkv, dtype=f32),
                         np.asarray(L0_attn_bproj, dtype=f32)])[None, :]
    b3 = np.ascontiguousarray(b3)

    x = np.asarray(L0_attn_x, dtype=f32)
    in_maps = []
    for c in range(NCORES):
        in_maps.append(dict(
            x=np.ascontiguousarray(x[:, c * IBC:(c + 1) * IBC, :]),
            wq=wq, ws=ws, wc=wc, wp=wp, b2=b2, b3=b3))
    return in_maps


def kernel(**inputs):
    in_maps = _prep_inputs(**inputs)
    nc = build_program()
    res = run_bass_kernel_spmd(nc, in_maps, core_ids=list(range(NCORES)))
    return np.concatenate([r["out"] for r in res.results], axis=1)


# version 22 (final)
# speedup vs baseline: 1.1058x; 1.1058x over previous
import numpy as np
from contextlib import ExitStack

import concourse.bass as bass
import concourse.mybir as mybir
from concourse.bass_utils import run_bass_kernel_spmd

F32 = mybir.dt.float32
F32R = mybir.dt.float32r
BF16 = mybir.dt.bfloat16
AF = mybir.ActivationFunctionType
ALU = mybir.AluOpType
AX = mybir.AxisListType

NCORES = 8
NSEQ = 4
IBC = 512
NIBT = 4
NT = NSEQ * NIBT
CM = 1024
KC = 8
NH = 16
HD = 64
SCALE = float(HD) ** -0.5

G_QKV, G_SEL, G_CMP, G_PROJ = 6, 16, 16, 2


class Sched:

    ENGINES = ("sp", "pe", "act", "dve", "pool")

    def __init__(self):
        self.items = {e: [] for e in self.ENGINES}
        self.ctr = {}
        self.seq = 0
        self.mark = {}

    DMA_ROT = {"dx": 8, "dw": 16, "dout": 8}

    def emit(self, eng, fn, waits=(), inc=None, inc_amt=1):
        tok = None
        if inc is not None:
            if inc in self.DMA_ROT:
                n = self.ctr.get(inc, 0)
                self.ctr[inc] = n + 1
                depth = self.DMA_ROT[inc]
                inc = f"{inc}{n % depth}"
                inc_amt = 16
                self.ctr[inc] = self.ctr.get(inc, 0) + 16
            else:
                self.ctr[inc] = self.ctr.get(inc, 0) + inc_amt
            tok = (inc, self.ctr[inc])
        ws = [w for w in waits if w is not None]
        self.items[eng].append((fn, ws, (inc, inc_amt) if inc else None,
                                self.seq))
        self.seq += 1
        return tok


def build_program(taps=False, serial=False, serial_until=0):
    nc = bass.Bass("TRN2", target_bir_lowering=False)

    x_h = nc.dram_tensor("x", [NSEQ, IBC, CM], F32, kind="ExternalInput").ap()
    wq_h = nc.dram_tensor("wq", [G_QKV, 128, 4096], F32, kind="ExternalInput").ap()
    ws_h = nc.dram_tensor("ws", [G_SEL, 128, 4096], F32, kind="ExternalInput").ap()
    wc_h = nc.dram_tensor("wc", [G_CMP, 128, 4096], F32, kind="ExternalInput").ap()
    wp_h = nc.dram_tensor("wp", [G_PROJ, 128, 4096], F32, kind="ExternalInput").ap()
    b2_h = nc.dram_tensor("b2", [16, 1024], F32, kind="ExternalInput").ap()
    b3_h = nc.dram_tensor("b3", [1, 4096], F32, kind="ExternalInput").ap()
    out_h = nc.dram_tensor("out", [NSEQ, IBC, CM], F32, kind="ExternalOutput").ap()
    if taps:
        tap_xn = nc.dram_tensor("tap_xn", [128, CM], F32, kind="ExternalOutput").ap()
        tap_qkv = nc.dram_tensor("tap_qkv", [128, 1536], F32, kind="ExternalOutput").ap()
        tap_a = nc.dram_tensor("tap_a", [128, 128], F32, kind="ExternalOutput").ap()
        tap_g = nc.dram_tensor("tap_g", [128, CM], F32, kind="ExternalOutput").ap()

    s = Sched()

    pad_prev = [None]
    plpad_prev = [None]

    def emit_pool_pad():
        tok = s.emit("pool", lambda e: e.memset(plpad[:], 1.0),
                     waits=[plpad_prev[0]], inc="pool")
        plpad_prev[0] = tok
        return tok

    def emit_pad():
        tok = s.emit("act", lambda e: e.copy(out=padt[:], in_=ones_f[:, 0:1]),
                     waits=[t_ones, pad_prev[0]], inc="act")
        pad_prev[0] = tok
        return tok

    def later(tok, n=1):
        return None if tok is None else (tok[0], tok[1] + n)

    es0 = ExitStack()
    es12 = ExitStack()

    ones_f = es0.enter_context(nc.sbuf_tensor("ones_f", [128, 128], F32))
    ones_r = es0.enter_context(nc.sbuf_tensor("ones_r", [128, 128], F32R))
    ident = es0.enter_context(nc.sbuf_tensor("ident", [128, 128], BF16))
    eps_t = es0.enter_context(nc.sbuf_tensor("eps_t", [128, 1], F32))
    padt = es0.enter_context(nc.sbuf_tensor("padt", [128, 1], F32))
    plpad = es0.enter_context(nc.sbuf_tensor("plpad", [128, 1], F32))
    bias_r = es0.enter_context(nc.sbuf_tensor("bias_r", [128, 3072], F32R))
    xnT = es0.enter_context(nc.sbuf_tensor("xnT", [128, KC, 2048], BF16))
    Fg = es0.enter_context(nc.sbuf_tensor("Fg", [128, NT, CM], BF16))
    wqb = es0.enter_context(nc.sbuf_tensor("wqb", [128, KC, 3072], BF16))
    stag3 = es0.enter_context(nc.sbuf_tensor("stag3", [128, 1024], F32))

    ps = [es0.enter_context(nc.psum_tensor(f"ps{i}", [128, 512], F32))
          for i in range(5)]
    pt2 = [es0.enter_context(nc.psum_tensor(f"pt{i}", [128, 1024], BF16))
           for i in range(3)]

    sems = {}
    sem_names = ["pe", "act", "dve", "pool", "chain"]
    for base, depth in Sched.DMA_ROT.items():
        sem_names += [f"{base}{i}" for i in range(depth)]
    for name in sem_names:
        sems[name] = es0.enter_context(nc.semaphore(name))

    wscb = es12.enter_context(nc.sbuf_tensor("wscb", [128, 2, 2, KC, 512], BF16))
    stag = es12.enter_context(nc.sbuf_tensor("stag", [128, 2048], F32))
    miscF = es12.enter_context(nc.sbuf_tensor("miscF", [128, 6144], F32))
    miscB = es12.enter_context(nc.sbuf_tensor("miscB", [128, 4096], BF16))
    bn6 = es12.enter_context(nc.sbuf_tensor("bn6", [128, 12], F32))
    mv = es12.enter_context(nc.sbuf_tensor("mv", [128, 2], F32))
    rstd = es12.enter_context(nc.sbuf_tensor("rstd", [128, 1], F32))
    wsb = wscb[:, 0]
    wcb = wscb[:, 1]
    xt = miscF[:, 0:2048].rearrange("p (b c) -> p b c", c=CM)
    bstag = miscF[:, 2048:3072]
    selM = miscF[:, 3072:3136]
    xnb = miscB[:, 0:1024]
    selK = miscB[:, 1024:1536]
    selP = miscB[:, 1536:2048]

    t_ones = s.emit("pool", lambda e: e.memset(ones_f[:], 1.0), inc="pool")
    t_eps = s.emit("pool", lambda e: e.memset(eps_t[:], 1e-5), inc="pool")
    t_identz = s.emit("pool", lambda e: e.memset(ident[:], 0.0), inc="pool")
    t_ident = s.emit("pool", lambda e: e.affine_select(
        out=ident[:], in_=ident[:], compare_op=ALU.not_equal, fill=1.0,
        base=0, pattern=[[-1, 128]], channel_multiplier=1),
        waits=[t_identz], inc="pool")
    t_onesr = s.emit("dve", lambda e: e.tensor_copy(out=ones_r[:], in_=ones_f[:]),
                     waits=[t_ones], inc="dve")



    s.mark["ph1"] = s.seq
    xnT_tok = [None] * NT
    xt_free = [None, None]
    tr_free = [None] * 3
    trc = 0

    def pt_slot(sl):
        return pt2[sl % 3][:, 0:128]
    prev_sq = None
    prev_xn = None
    last_tr_pe = None
    for t in range(NT):
        nn, ibt = t // NIBT, t % NIBT
        par = t % 2
        d = s.emit("sp", lambda e, nn=nn, ibt=ibt, par=par: e.dma_start(
            out=xt[:, par, :], in_=x_h[nn, ibt * 128:(ibt + 1) * 128, :]),
            waits=[xt_free[par]], inc="dx", inc_amt=16)
        s.emit("dve", lambda e, par=par: e.bn_stats(bn6[:, 0:6], xt[:, par, 0:512]),
               waits=[d], inc="dve")
        t_st2 = s.emit("dve", lambda e, par=par: e.bn_stats(
            bn6[:, 6:12], xt[:, par, 512:1024]), inc="dve")
        t_ag = s.emit("dve", lambda e: e.bn_aggr(mv[:], bn6[:]),
                      waits=[prev_sq, t_st2], inc="dve")
        t_sq = s.emit("act", lambda e: e.activation(rstd[:], mv[:, 1:2], AF.Sqrt,
                                                    bias=eps_t[:]),
                      waits=[t_ag, prev_xn, t_eps], inc="act")
        prev_sq = t_sq
        t_rc = s.emit("dve", lambda e: e.reciprocal(rstd[:], rstd[:]),
                      waits=[t_sq], inc="dve")
        t_xn = s.emit("dve", lambda e, par=par: e.tensor_scalar(
            out=xnb[:], in0=xt[:, par, :], scalar1=mv[:, 0:1], scalar2=rstd[:],
            op0=ALU.subtract, op1=ALU.mult), waits=[last_tr_pe, t_rc],
            inc="dve")
        prev_xn = t_xn
        xt_free[par] = t_xn
        for c in range(KC):
            sl = trc % 3
            t_tr = s.emit("pe", lambda e, c=c, sl=sl: e.transpose(
                pt_slot(sl), xnb[:, c * 128:(c + 1) * 128], ident[:]),
                waits=[t_xn, t_ident, tr_free[sl]], inc="pe")
            last_tr_pe = t_tr
            xnT_tok[t] = s.emit("act", lambda e, c=c, t=t, sl=sl: e.copy(
                out=xnT[:, c, t * 128:(t + 1) * 128], in_=pt_slot(sl)),
                waits=[later(t_tr, 2)], inc="act")
            tr_free[sl] = emit_pad()
            trc += 1
        for _ in range(2):
            sl = trc % 3
            last_tr_pe = s.emit("pe", lambda e, sl=sl: e.transpose(
                pt_slot(sl), ident[:], ident[:]),
                waits=[tr_free[sl]], inc="pe")
            trc += 1
    tap_toks = []
    if taps:
        tap_toks.append(s.emit("pool", lambda e: e.dma_start(
            out=tap_xn, in_=xnb[:]),
            waits=[xnT_tok[NT - 1]], inc="dout", inc_amt=16))

    s.mark["ph2"] = s.seq
    st3_free = None

    def load_w3_quarter(dst, srch, g, q, stag, dg=None, extra=()):
        nonlocal st3_free
        if dg is None:
            dg = g
        d = s.emit("sp", lambda e, g=g, q=q: e.dma_start(
            out=stag[:], in_=srch[g][:, q * 1024:(q + 1) * 1024]),
            waits=[st3_free] + list(extra), inc="dw", inc_amt=16)
        tok = s.emit("act", lambda e, dg=dg, q=q: e.copy(
            out=dst[:, q * 2:(q + 1) * 2, dg * 512:(dg + 1) * 512],
            in_=stag[:].rearrange("p (k j) -> p k j", j=512)),
            waits=[d], inc="act")
        st3_free = emit_pad()
        return tok

    wq_tok = [None] * G_QKV
    wq_quarters = [(g, q) for g in range(G_QKV) for q in range(4)]

    pe_g_tok = {}
    sel_bank_free = [None, None]
    stag_free = None
    bstag_free = None
    f_tok = [None] * NT

    for g in range(G_SEL):
        wpar = g % 2
        db = s.emit("sp", lambda e, g=g: e.dma_start(out=bstag[0:1, :],
                                                     in_=b2_h[g:g + 1, :]),
                    waits=[bstag_free], inc="dw", inc_amt=16)
        cb = s.emit("dve", lambda e, wpar=wpar: e.tensor_copy(
            out=bias_r[0:1, wpar * 1024:wpar * 1024 + 1024],
            in_=bstag[0:1, :]),
            waits=[db, pe_g_tok.get(g - 2)], inc="dve")
        bstag_free = later(cb)
        c1 = c2 = None
        for mat, wh, wbuf in ((0, ws_h, wsb), (1, wc_h, wcb)):
            for half in range(2):
                d1 = s.emit("sp", lambda e, g=g, half=half, wh=wh: e.dma_start(
                    out=stag[:], in_=wh[g][:, half * 2048:(half + 1) * 2048]),
                    waits=[stag_free], inc="dw", inc_amt=16)
                cc = s.emit("act", lambda e, wpar=wpar, half=half, wbuf=wbuf: e.copy(
                    out=wbuf[:, wpar, half * 4:(half + 1) * 4, :],
                    in_=stag[:].rearrange("p (k j) -> p k j", j=512)),
                    waits=[d1, pe_g_tok.get(g - 2)], inc="act")
                stag_free = emit_pad()
            if mat == 0:
                c1 = cc
            else:
                c2 = cc
        for gq in wq_quarters[2 * g:2 * g + 2]:
            wq_tok[gq[0]] = load_w3_quarter(wqb, wq_h, gq[0], gq[1], stag3)

        cS = slice(wpar * 1024, wpar * 1024 + 512)
        cC = slice(wpar * 1024 + 512, wpar * 1024 + 1024)
        for t in range(NT):
            gi = g * NT + t
            par = gi % 2
            psS, psC = ps[2 * par], ps[2 * par + 1]
            for k in range(KC):
                tok = s.emit("pe", lambda e, k=k, t=t, wpar=wpar, psS=psS: e.matmul(
                    psS[:], xnT[:, k, t * 128:(t + 1) * 128], wsb[:, wpar, k, :],
                    start=(k == 0), stop=False),
                    waits=[later(c1), later(xnT_tok[t]),
                           later(sel_bank_free[par]), t_onesr]
                    if k == 0 else (), inc="pe")
            tS = s.emit("pe", lambda e, cS=cS, psS=psS: e.matmul(
                psS[:], ones_r[0:1, :], bias_r[0:1, cS],
                start=False, stop=True), waits=[cb], inc="pe")
            for k in range(KC):
                tok = s.emit("pe", lambda e, k=k, t=t, wpar=wpar, psC=psC: e.matmul(
                    psC[:], xnT[:, k, t * 128:(t + 1) * 128], wcb[:, wpar, k, :],
                    start=(k == 0), stop=False),
                    waits=[later(c2)] if k == 0 else (), inc="pe")
            tC = s.emit("pe", lambda e, cC=cC, psC=psC: e.matmul(
                psC[:], ones_r[0:1, :], bias_r[0:1, cC],
                start=False, stop=True), inc="pe")
            pe_g_tok[g] = tC
            t_M = s.emit("dve", lambda e, psS=psS: e.tensor_reduce(
                out=selM[:], in_=psS[:].rearrange("p (o w) -> p o w", w=8),
                axis=AX.X, op=ALU.max), waits=[later(tS)], inc="dve")
            t_K = s.emit("dve", lambda e, psS=psS: e.tensor_tensor(
                out=selK[:].rearrange("p (o w) -> p o w", w=8),
                in0=psS[:].rearrange("p (o w) -> p o w", w=8),
                in1=selM[:].unsqueeze(2).broadcast_to((128, 64, 8)),
                op=ALU.is_ge), waits=[t_M], inc="dve")
            t_P = s.emit("dve", lambda e, psC=psC: e.scalar_tensor_tensor(
                out=selP[:], in0=selK[:], scalar=1.0,
                in1=psC[:], op0=ALU.mult, op1=ALU.mult),
                waits=[later(tC), t_K], inc="dve")
            def _fred(e, t=t, g=g):
                with nc.allow_low_precision(reason="hard-selection gate in bf16"):
                    return e.tensor_reduce(
                        out=Fg[:, t, g * 64:(g + 1) * 64],
                        in_=selP[:].rearrange("p (o w) -> p o w", w=8),
                        axis=AX.X, op=ALU.add)
            tF = s.emit("dve", _fred, waits=[t_P], inc="dve")
            sel_bank_free[par] = tF
            f_tok[t] = tF

    sl = trc % 3
    s.emit("pe", lambda e, sl=sl: e.transpose(
        pt_slot(sl), ident[:], ident[:]), waits=[tr_free[sl]], inc="pe")
    trc += 1

    g_tok = [None] * NT
    for t in range(NT):
        g_tok[t] = s.emit("act", lambda e, t=t: e.activation(
            Fg[:, t, :], Fg[:, t, :], AF.Silu), waits=[f_tok[t]], inc="act")
    if taps:
        s.emit("pool", lambda e: e.dma_start(out=tap_g, in_=Fg[:, 0, :]),
               waits=[g_tok[0]], inc="dout", inc_amt=16)

    ph2_done = [pe_g_tok[G_SEL - 1], sel_bank_free[0], sel_bank_free[1],
                g_tok[NT - 1]]

    s.mark["ph3"] = s.seq
    es12.close()
    es3a = ExitStack()
    es3b = ExitStack()
    attnT = es3b.enter_context(nc.sbuf_tensor("attnT", [128, KC, 2048], BF16))
    qkv = es3a.enter_context(nc.sbuf_tensor("qkv", [128, 2, NSEQ, 1536], BF16))
    aout = es3a.enter_context(nc.sbuf_tensor("aout", [128, 2, NSEQ, 512], BF16))
    scF = es3a.enter_context(nc.sbuf_tensor("scF", [128, 512], F32))
    scA = es3a.enter_context(nc.sbuf_tensor("scA", [128, 128], BF16))
    pav = es3a.enter_context(nc.sbuf_tensor("pav", [128, 6144], BF16))
    sc = scF[:, 0:128]
    scD = scF[:, 128:256]
    scE = scF[:, 256:384]
    scM = scF[:, 384:416]
    scZ = scF[:, 416:448]
    scR = scF[:, 448:480]
    prod = [pav[:, 0:2048], pav[:, 2048:4096]]
    avm = pav[:, 4096:6144]

    t_bias3 = None
    for r in range(3):
        d = s.emit("sp", lambda e, r=r: e.dma_start(
            out=stag3[0:1, :], in_=b3_h[0:1, r * 1024:(r + 1) * 1024]),
            waits=([t_bias3] if r else []) + [st3_free],
            inc="dw", inc_amt=16)
        t_bias3 = s.emit("dve", lambda e, r=r: e.tensor_copy(
            out=bias_r[:, r * 1024:(r + 1) * 1024], in_=stag3[:]),
            waits=[d, pe_g_tok[G_SEL - 1]], inc="dve")
        st3_free = t_bias3

    rot = [sel_bank_free[0], sel_bank_free[0],
           sel_bank_free[1], sel_bank_free[1], None]
    rotc = 0
    qkv_free = [None, None]
    aout_free = [None, None]
    prod_free = [None, None]
    avm_free = None
    scE_free = [None] * NSEQ
    prev_tE = [None] * NSEQ
    wp_tok = [None] * G_PROJ
    pending_tr = []
    last_copy = None
    first_pool = list(sel_bank_free)

    H2 = 8
    sc4 = sc[:].rearrange("p (n h m) -> p n h m", n=NSEQ, h=H2, m=NSEQ)
    scA4 = scA[:].rearrange("p (n h m) -> p n h m", n=NSEQ, h=H2, m=NSEQ)

    def flush_transposes():
        nonlocal pending_tr, trc, last_copy
        if not pending_tr:
            return
        it_f = pending_tr[0][3]
        for ta, t, hh_, it_ in pending_tr:
            n_ = t // NIBT
            for j in range(4):
                sl = trc % 3
                t_tr = s.emit("pe", lambda e, it_=it_, n_=n_, j=j, sl=sl:
                              e.transpose(
                    pt_slot(sl),
                    aout[:, it_, n_, j * 128:(j + 1) * 128],
                    ident[:]), waits=[ta, tr_free[sl]], inc="pe")
                last_copy = s.emit(
                    "act", lambda e, t=t, j=j, hh_=hh_, sl=sl: e.copy(
                        out=attnT[:, hh_ * 4 + j, t * 128:(t + 1) * 128],
                        in_=pt_slot(sl)), waits=[later(t_tr, 2)],
                    inc="act")
                tr_free[sl] = emit_pad()
                trc += 1
        for _ in range(2):
            sl = trc % 3
            s.emit("pe", lambda e, sl=sl: e.transpose(
                pt_slot(sl), ident[:], ident[:]),
                waits=[tr_free[sl]], inc="pe")
            trc += 1
        aout_free[it_f] = last_copy
        pending_tr = []

    for ibt in range(NIBT):
        for hh in range(2):
            it = (ibt * 2 + hh) % 2
            e_k_last = None
            e_q = [None] * NSEQ
            for comp in (1, 0, 2):
                fo = comp * 2 + hh
                for nn in range(NSEQ):
                    t = nn * NIBT + ibt
                    bank = rotc % 5
                    rotc += 1
                    for k in range(KC):
                        s.emit("pe", lambda e, k=k, t=t, fo=fo, bank=bank: e.matmul(
                            ps[bank][:], xnT[:, k, t * 128:(t + 1) * 128],
                            wqb[:, k, fo * 512:(fo + 1) * 512],
                            start=(k == 0), stop=False),
                            waits=[later(wq_tok[fo]), later(rot[bank])]
                            if k == 0 else (), inc="pe")
                    tq_ = s.emit("pe", lambda e, fo=fo, bank=bank: e.matmul(
                        ps[bank][:], ones_r[0:1, :],
                        bias_r[0:1, fo * 512:(fo + 1) * 512],
                        start=False, stop=True), waits=[t_bias3], inc="pe")
                    ev = rot[bank] = s.emit(
                        "act", lambda e, it=it, nn=nn, comp=comp, bank=bank: e.copy(
                            out=qkv[:, it, nn, comp * 512:(comp + 1) * 512],
                            in_=ps[bank][:]),
                        waits=[later(tq_, 2)] + (qkv_free[it] or []), inc="act")
                    if comp == 1:
                        e_k_last = ev
                    elif comp == 0:
                        e_q[nn] = ev
            evac_last = rot[(rotc - 1) % 5]
            if ibt == NIBT - 1 and hh == 0:
                last_mm_evenfo = tq_

            flush_transposes()

            emit_pad()
            last_qk_pool = None
            td = None
            for n_ in range(NSEQ):
                k_all = qkv[:, it, :, 512:1024].rearrange(
                    "p m (h d) -> p m h d", d=HD)
                q_n = qkv[:, it, n_, 0:512].rearrange(
                    "p (h d) -> p h d", d=HD).unsqueeze(1).broadcast_to(
                    (128, NSEQ, H2, HD))
                pp = n_ % 2
                tq = s.emit("pool", lambda e, pp=pp, k_all=k_all, q_n=q_n:
                            e.tensor_tensor(
                    out=prod[pp].rearrange("p (m h d) -> p m h d",
                                           m=NSEQ, d=HD),
                    in0=k_all, in1=q_n, op=ALU.mult),
                    waits=[later(e_k_last), later(e_q[n_]), prod_free[pp]]
                    + first_pool + tap_toks, inc="pool")
                first_pool = []
                last_qk_pool = tq
                td = s.emit("dve", lambda e, pp=pp, n_=n_: e.tensor_reduce(
                    out=sc4[:, n_, :, :].transpose([0, 2, 1]),
                    in_=prod[pp].rearrange("p (m h d) -> p m h d",
                                           m=NSEQ, d=HD),
                    axis=AX.X, op=ALU.add), waits=[later(tq)], inc="dve")
                prod_free[pp] = td
                if n_ == NSEQ - 1:
                    emit_pool_pad()

            scw = sc[:].rearrange("p (a m) -> p a m", m=NSEQ)
            t_M4 = s.emit("dve", lambda e: e.tensor_reduce(
                out=scM[:], in_=scw,
                axis=AX.X, op=ALU.max), waits=[td], inc="dve")
            tD = s.emit("dve", lambda e: e.tensor_tensor(
                out=scD[:].rearrange("p (a m) -> p a m", m=NSEQ),
                in0=scw,
                in1=scM[:].unsqueeze(2).broadcast_to((128, 32, NSEQ)),
                op=ALU.subtract),
                waits=[prev_tE[0], t_M4], inc="dve")
            tE = s.emit("act", lambda e: e.activation(
                scE[:], scD[:], AF.Exp, scale=SCALE),
                waits=[tD, scE_free[0]], inc="act")
            tEp = emit_pad()
            prev_tE[0] = tEp
            t_Z = s.emit("dve", lambda e: e.tensor_reduce(
                out=scZ[:],
                in_=scE[:].rearrange("p (a m) -> p a m", m=NSEQ),
                axis=AX.X, op=ALU.add), waits=[tEp], inc="dve")
            t_R = s.emit("dve", lambda e: e.reciprocal(
                scR[:], scZ[:]), waits=[t_Z], inc="dve")
            tA = s.emit("dve", lambda e: e.tensor_tensor(
                out=scA[:].rearrange("p (a m) -> p a m", m=NSEQ),
                in0=scE[:].rearrange("p (a m) -> p a m", m=NSEQ),
                in1=scR[:].unsqueeze(2).broadcast_to((128, 32, NSEQ)),
                op=ALU.mult), waits=[t_R], inc="dve")
            scE_free[0] = tA

            for n_ in range(NSEQ):
                v_all = qkv[:, it, :, 1024:1536].rearrange(
                    "p m (h d) -> p m h d", d=HD)
                a_n = scA4[:, n_, :, :].transpose([0, 2, 1]).unsqueeze(
                    3).broadcast_to((128, NSEQ, H2, HD))
                tm = s.emit("dve", lambda e, v_all=v_all, a_n=a_n:
                            e.tensor_tensor(
                    out=avm[:].rearrange("p (m h d) -> p m h d",
                                         m=NSEQ, d=HD),
                    in0=v_all, in1=a_n, op=ALU.mult),
                    waits=[avm_free, tA, later(evac_last)], inc="dve")

                def _avred(e, it=it, n_=n_):
                    with nc.allow_low_precision(reason="attention out bf16"):
                        return e.tensor_reduce(
                            out=aout[:, it, n_, :],
                            in_=avm[:].rearrange("p (m f) -> p f m", f=512),
                            axis=AX.X, op=ALU.add)
                ta = s.emit("dve", _avred, waits=[tm, later(aout_free[it])],
                            inc="dve")
                avm_free = ta
                t = n_ * NIBT + ibt
                pending_tr.append((ta, t, hh, it))
            qkv_free[it] = [later(last_qk_pool), avm_free]
            if taps and ibt == 0 and hh == 0:
                tap_toks.append(s.emit("pool", lambda e: e.dma_start(
                    out=tap_qkv, in_=qkv[:, 0, 0, :]),
                    waits=[avm_free], inc="dout", inc_amt=16))
                tap_toks.append(s.emit("pool", lambda e: e.dma_start(
                    out=tap_a, in_=scA[:]),
                    waits=[avm_free], inc="dout", inc_amt=16))
                qkv_free[it] = qkv_free[it] + tap_toks
                scE_free = [(t if t is None else max(t, tap_toks[-1],
                             key=lambda x: 0 if x is None else 1))
                            for t in scE_free]
                prev_tE = [tap_toks[-1] if t is None else t for t in prev_tE]
            if ibt == NIBT - 1 and hh == 1:
                for g in range(G_PROJ):
                    for q in range(4):
                        wp_tok[g] = load_w3_quarter(
                            wqb, wp_h, g, q, stag3, dg=2 * g,
                            extra=[later(last_mm_evenfo, 2)])

    flush_transposes()
    attnT_tok = [last_copy]
    ph3_done = [later(last_copy), qkv_free[0][0], qkv_free[0][1],
                qkv_free[1][0], qkv_free[1][1]]

    s.mark["ph4"] = s.seq
    es3a.close()
    es4 = ExitStack()
    xre = es4.enter_context(nc.sbuf_tensor("xre", [128, 2, CM], F32))
    t0b = es4.enter_context(nc.sbuf_tensor("t0b", [128, 512], F32))
    outs = es4.enter_context(nc.sbuf_tensor("outs", [128, 2, CM], F32))

    d = s.emit("sp", lambda e: e.dma_start(
        out=stag3[0:1, :], in_=b3_h[0:1, 3072:4096]),
        waits=ph3_done + [st3_free], inc="dw", inc_amt=16)
    t_bias4 = s.emit("dve", lambda e: e.tensor_copy(
        out=bias_r[:, 0:1024], in_=stag3[:]), waits=[d], inc="dve")
    st3_free = t_bias4

    xre_free = [None, None]
    outs_free = [None, None]
    t0b_prev = [None]
    for t in range(NT):
        nn, ibt = t // NIBT, t % NIBT
        par = t % 2
        dx = s.emit("sp", lambda e, nn=nn, ibt=ibt, par=par: e.dma_start(
            out=xre[:, par, :], in_=x_h[nn, ibt * 128:(ibt + 1) * 128, :]),
            waits=[xre_free[par]], inc="dx", inc_amt=16)
        for gp in range(2):
            bank = rotc % 5
            rotc += 1
            for c in range(KC):
                s.emit("pe", lambda e, c=c, t=t, gp=gp, bank=bank: e.matmul(
                    ps[bank][:], attnT[:, c, t * 128:(t + 1) * 128],
                    wqb[:, c, gp * 1024:gp * 1024 + 512],
                    start=(c == 0), stop=False),
                    waits=[later(x) for x in attnT_tok]
                    + [later(wp_tok[gp]), later(rot[bank])]
                    if c == 0 else (), inc="pe")
            tp_ = s.emit("pe", lambda e, gp=gp, bank=bank: e.matmul(
                ps[bank][:], ones_r[0:1, :],
                bias_r[0:1, gp * 512:(gp + 1) * 512],
                start=False, stop=True), waits=[t_bias4], inc="pe")
            t_stt = s.emit("dve", lambda e, t=t, gp=gp, bank=bank: e.scalar_tensor_tensor(
                out=t0b[:], in0=Fg[:, t, gp * 512:(gp + 1) * 512],
                scalar=1.0, in1=ps[bank][:], op0=ALU.mult, op1=ALU.mult),
                waits=[later(tp_), later(g_tok[t]), t0b_prev[0]], inc="dve")
            rot[bank] = tout = s.emit("dve", lambda e, par=par, gp=gp: e.tensor_tensor(
                out=outs[:, par, gp * 512:(gp + 1) * 512],
                in0=xre[:, par, gp * 512:(gp + 1) * 512], in1=t0b[:],
                op=ALU.add), waits=[dx, outs_free[par], t_stt], inc="dve")
            t0b_prev[0] = tout
        xre_free[par] = tout
        od = s.emit("act", lambda e, nn=nn, ibt=ibt, par=par: e.dma_start(
            out=out_h[nn, ibt * 128:(ibt + 1) * 128, :], in_=outs[:, par, :]),
            waits=[tout], inc="dout", inc_amt=16)
        outs_free[par] = od

    emit_pool_pad()
    sl = trc % 3
    s.emit("pe", lambda e, sl=sl: e.transpose(pt_slot(sl), ident[:], ident[:]),
           waits=[tr_free[sl]], inc="pe")

    s.mark["end"] = s.seq
    douts = [(f"dout{i}", s.ctr[f"dout{i}"]) for i in range(s.DMA_ROT["dout"])
             if f"dout{i}" in s.ctr]
    s.emit("sp", lambda e: e.nop(), waits=douts)

    if serial:
        serial_until = s.seq + 1
    pre_ctr = {}
    for eng_name in s.ENGINES:
        for fn, waits, inc, seq in s.items[eng_name]:
            if seq < serial_until and inc is not None:
                pre_ctr[inc[0]] = pre_ctr.get(inc[0], 0) + inc[1]
    with nc.Block() as block:
        def make_body(eng_name):
            def body(eng):
                last_wait = {}
                for fn, waits, inc, seq in s.items[eng_name]:
                    if seq < serial_until:
                        if seq > 0:
                            eng.wait_ge(sems["chain"], 16 * seq)
                        inst = fn(eng)
                        inst.then_inc(sems["chain"], 16)
                        continue
                    if serial_until and last_wait.get("chain", 0) == 0:
                        eng.wait_ge(sems["chain"], 16 * serial_until)
                        last_wait["chain"] = 16 * serial_until
                    for sem_name, val in waits:
                        val -= pre_ctr.get(sem_name, 0)
                        if val > last_wait.get(sem_name, 0):
                            eng.wait_ge(sems[sem_name], val)
                            last_wait[sem_name] = val
                    inst = fn(eng)
                    if inc is not None:
                        inst.then_inc(sems[inc[0]], inc[1])
            return body
        block.sync(make_body("sp"))
        block.tensor(make_body("pe"))
        block.scalar(make_body("act"))
        block.vector(make_body("dve"))
        block.gpsimd(make_body("pool"))

    es4.close()
    es3b.close()
    es0.close()
    nc._sched = s
    return nc


def _prep_inputs(L0_attn_x, L0_attn_Wqkv, L0_attn_bqkv, L0_attn_Wproj,
                 L0_attn_bproj, L0_ffn_sel_w, L0_ffn_sel_b, L0_ffn_comp_w,
                 L0_ffn_comp_b):
    f32 = np.float32

    def pieces(W):
        WT = np.ascontiguousarray(np.asarray(W, dtype=f32).T)
        G = WT.shape[1] // 512
        arr = WT.reshape(KC, 128, G, 512).transpose(2, 1, 0, 3)
        return np.ascontiguousarray(arr.reshape(G, 128, 4096))

    wq = pieces(L0_attn_Wqkv)
    ws = pieces(L0_ffn_sel_w)
    wcat = np.asarray(L0_ffn_comp_w, dtype=f32).transpose(1, 0, 2).reshape(8192, 1024)
    wc = pieces(wcat)
    wp = pieces(L0_attn_Wproj)

    selb = np.asarray(L0_ffn_sel_b, dtype=f32)
    cmpb = np.asarray(L0_ffn_comp_b, dtype=f32).T.reshape(-1)
    b2 = np.stack([np.concatenate([selb[g * 512:(g + 1) * 512],
                                   cmpb[g * 512:(g + 1) * 512]])
                   for g in range(16)])
    b3 = np.concatenate([np.asarray(L0_attn_bqkv, dtype=f32),
                         np.asarray(L0_attn_bproj, dtype=f32)])[None, :]
    b3 = np.ascontiguousarray(b3)

    x = np.asarray(L0_attn_x, dtype=f32)
    in_maps = []
    for c in range(NCORES):
        in_maps.append(dict(
            x=np.ascontiguousarray(x[:, c * IBC:(c + 1) * IBC, :]),
            wq=wq, ws=ws, wc=wc, wp=wp, b2=b2, b3=b3))
    return in_maps


def kernel(**inputs):
    in_maps = _prep_inputs(**inputs)
    nc = build_program()
    res = run_bass_kernel_spmd(nc, in_maps, core_ids=list(range(NCORES)))
    return np.concatenate([r["out"] for r in res.results], axis=1)
